# version 17
# baseline (speedup 1.0000x reference)
import numpy as np
from contextlib import ExitStack

import concourse.bass as bass
import concourse.mybir as mybir
from concourse.bass import IndirectOffsetOnAxis
from concourse.bass_utils import run_bass_kernel_spmd

F32 = mybir.dt.float32
I32 = mybir.dt.int32
AF = mybir.ActivationFunctionType
ALU = mybir.AluOpType
AX = mybir.AxisListType

B, T, D, H, DH, E, INNER = 8, 2048, 1024, 16, 64, 8, 4096
NT = T // 128
NCC = D // 128
NI = INNER // 128
NEGBIG = -1.0e30
SET_I = 256
NSETS = INNER // SET_I
CAP = 512
NSLOT = E * CAP
LN_EPS = 1e-5
GATE_EPS = 1e-6


class Sched:
    ENGS = ("tensor", "vector", "scalar", "sync", "gpsimd")

    def __init__(self):
        self.ops = {e: [] for e in self.ENGS}
        self.cnt = {}
        self.serial = {}

    def reg(self, sem):
        self.cnt[sem] = 0
        return sem

    def emit(self, eng, fn, waits=(), inc=None, inc_by=1):
        waits = list(waits)
        ssem = self.serial.get(eng)
        if ssem is not None:
            waits.append((ssem, self.cnt[ssem]))
            if inc is None:
                inc, inc_by = ssem, 1
        waits = tuple((s, v) for (s, v) in waits if v > 0)

        def op(e):
            for (s, v) in waits:
                e.wait_ge(s, v)
            inst = fn(e)
            if inc is not None:
                inst.then_inc(inc, inc_by)

        self.ops[eng].append(op)
        if inc is not None:
            self.cnt[inc] += inc_by
        return self.cnt[inc] if inc is not None else None

    def at(self, sem):
        return self.cnt[sem]


def build_nc():
    nc = bass.Bass("TRN2", target_bir_lowering=False,
                   detect_race_conditions=False)

    x_own = nc.dram_tensor("x_own", [T, D], F32, kind="ExternalInput").ap()
    xTf = nc.dram_tensor("xTf", [D, B * T], F32, kind="ExternalInput").ap()
    wqT = nc.dram_tensor("wqT", [D, D], F32, kind="ExternalInput").ap()
    wkT = nc.dram_tensor("wkT", [D, D], F32, kind="ExternalInput").ap()
    wvT = nc.dram_tensor("wvT", [D, D], F32, kind="ExternalInput").ap()
    woT = nc.dram_tensor("woT", [D, D], F32, kind="ExternalInput").ap()
    wg9_d = nc.dram_tensor("wg9", [D, 9], F32, kind="ExternalInput").ap()
    w1 = nc.dram_tensor("w1", [E, D, INNER], F32, kind="ExternalInput").ap()
    w2 = nc.dram_tensor("w2", [E, INNER, D], F32, kind="ExternalInput").ap()
    bgb_d = nc.dram_tensor("bgb", [128, E], F32, kind="ExternalInput").ap()
    ident_d = nc.dram_tensor("ident", [128, 128], F32, kind="ExternalInput").ap()
    ltriT_d = nc.dram_tensor("ltriT", [128, 128], F32, kind="ExternalInput").ap()
    iota8_d = nc.dram_tensor("iota8", [128, E], F32, kind="ExternalInput").ap()
    onesc_d = nc.dram_tensor("onesc", [128, 1], F32, kind="ExternalInput").ap()
    onesr_d = nc.dram_tensor("onesr", [1, 128], F32, kind="ExternalInput").ap()
    e512_d = nc.dram_tensor("e512r", [1, E], F32, kind="ExternalInput").ap()
    biasm_d = nc.dram_tensor("biasm", [H, 128, 2176], F32, kind="ExternalInput").ap()
    out_d = nc.dram_tensor("out", [T, D], F32, kind="ExternalOutput").ap()
    xgp = nc.dram_tensor("xgp", [NSLOT, D], F32).ap()
    yperm = nc.dram_tensor("yperm", [NSLOT, D], F32).ap()

    S = Sched()
    es = ExitStack()
    sbt = lambda name, cols: es.enter_context(nc.sbuf_tensor(name, [128, cols], F32))

    xnT = sbt("xnT", NCC * T)
    big2 = sbt("big2", 16384)
    wbuf = sbt("wbuf", 4096)
    mid = sbt("mid", 10240)
    sm1 = sbt("sm1", 1024)
    w2b = sbt("w2b", 2 * D)
    bb = sbt("bb", 2176)
    sloti = es.enter_context(nc.sbuf_tensor("sloti", [128, NT], I32))

    IDENT, LTRIT = 0, 128
    IOTA8, BGB, NEGS, RDB, ONESC, WG9 = 256, 264, 272, 280, 288, 289
    MSKA, GATEC, IDXC, SLOTF = 361, 489, 505, 521
    SCR = 617
    ONESR = 657
    DENR = 785
    DRS = 793
    MST, IST = 921, 937
    NMX, RSUM, RSC = 953, 954, 955
    E512R, BASER = 956, 964

    M_PST, M_XGT, M_YST = 0, 4096, 8192
    A_QT, A_KT, A_P, A_PT, A_V = 0, 2048, 4096, 6144, 8192
    F_YS, F_XREL, F_OUT, F_G = 0, 2048, 4096, 6144
    XG_STG = 8192

    ps0 = es.enter_context(nc.psum_tensor("ps0", [128, 2048], F32))
    ps1 = es.enter_context(nc.psum_tensor("ps1", [128, 2048], F32))

    sem = lambda n: S.reg(es.enter_context(nc.semaphore(n)))
    qC = sem("qC")
    qX = sem("qX")
    qA = [sem("qA0"), sem("qA1")]
    qWa = [sem("qWa0"), sem("qWa1")]
    qW1 = [sem("qW1a"), sem("qW1b")]
    qW2 = [sem("qW2a"), sem("qW2b")]
    qS = [sem("qS0"), sem("qS1")]
    qP = [sem("qP0"), sem("qP1")]
    qY = [sem("qY0"), sem("qY1")]
    qR = [sem("qR0"), sem("qR1")]
    qO = [sem("qO0"), sem("qO1")]
    tE = sem("tE")
    vE = sem("vE")
    sE = sem("sE")
    S.serial = {"vector": vE, "scalar": sE}

    def mm(out, lhsT, rhs, start, stop, waits=(), inc=None, is_transpose=None):
        kw = {}
        if is_transpose:
            kw["is_transpose"] = True
        S.emit("tensor",
               lambda e, o=out, l=lhsT, r=rhs, st=start, sp=stop, k=kw: e.matmul(
                   o, l, r, start=st, stop=sp, **k),
               waits=waits, inc=inc)

    S.emit("gpsimd", lambda e: e.memset(mid[:, 0:2048], 0.0))
    for k in range(NSLOT // 256):
        S.emit("gpsimd",
               lambda e, kk=k: e.dma_start(
                   out=xgp[kk * 256:(kk + 1) * 256, :].rearrange(
                       "(t p) d -> p t d", p=128),
                   in_=mid[:, 0:2048].rearrange("p (t d) -> p t d", t=2)),
               inc=qC, inc_by=16)
    for dst, src in [
        (sm1[:, IDENT:IDENT + 128], ident_d[:, :]),
        (sm1[:, LTRIT:LTRIT + 128], ltriT_d[:, :]),
        (sm1[:, IOTA8:IOTA8 + 8], iota8_d[:, :]),
        (sm1[:, BGB:BGB + 8], bgb_d[:, :]),
        (sm1[:, ONESC:ONESC + 1], onesc_d[:, :]),
        (sm1[0:1, ONESR:ONESR + 128], onesr_d[:, :]),
        (sm1[0:1, E512R:E512R + 8], e512_d[:, :]),
        (sm1[:, WG9:WG9 + 72].rearrange("p (cc e) -> p cc e", cc=8),
         wg9_d.rearrange("(cc p) e -> p cc e", p=128)),
    ]:
        S.emit("gpsimd", lambda e, d=dst, s=src: e.dma_start(out=d, in_=s),
               inc=qC, inc_by=16)
    for tch in range(NT):
        S.emit("gpsimd",
               lambda e, t=tch: e.dma_start(
                   out=big2[:, t * 1024:(t + 1) * 1024],
                   in_=x_own[t * 128:(t + 1) * 128, :]),
               inc=qC, inc_by=16)
    qC_all = S.at(qC)

    for cc in range(NCC):
        mm(ps1[0:1, 1024:1032], sm1[:, ONESC:ONESC + 1],
           sm1[:, WG9 + cc * 9:WG9 + cc * 9 + 8],
           start=(cc == 0), stop=(cc == NCC - 1),
           waits=[(qC, qC_all)] if cc == 0 else (),
           inc=tE if cc == NCC - 1 else None)
    tv = S.at(tE)
    S.emit("vector",
           lambda e: e.tensor_scalar(sm1[0:1, DENR:DENR + 8], ps1[0:1, 1024:1032],
                                     -1.0, None, ALU.mult),
           waits=[(tE, tv)])
    vv = S.at(vE)
    mm(ps1[:, 1536:1544], sm1[0:1, ONESR:ONESR + 128], sm1[0:1, DENR:DENR + 8],
       start=True, stop=True, waits=[(vE, vv)], inc=tE)
    tv = S.at(tE)
    S.emit("vector", lambda e: e.tensor_copy(sm1[:, NEGS:NEGS + 8], ps1[:, 1536:1544]),
           waits=[(tE, tv)])
    v_negs = S.at(vE)

    n_denom = 0
    last_mm_sb = 0
    last_v_sb = 0
    last_sq_sb = 0
    prev_chunk_free = {}
    pair_evac = {0: v_negs, 1: v_negs}

    for sb in range(B):
        for cc in range(NCC):
            if sb == 0:
                w = [(qC, qC_all)] if cc == 0 else ()
            else:
                w = [(tE, prev_chunk_free[cc]), (sE, last_sq_sb)]
            S.emit(
                "gpsimd",
                lambda e, s=sb, c=cc: e.dma_start(
                    out=xnT[:, c * T:(c + 1) * T],
                    in_=xTf[c * 128:(c + 1) * 128, s * T:(s + 1) * T]),
                waits=w, inc=qX, inc_by=16)
        qx_tot = S.at(qX)
        bmm_done = {}
        for cc in range(NCC):
            xq = (cc % 2) * 2048
            w = [(qX, qx_tot)]
            if cc >= 2:
                w.append((tE, bmm_done[cc - 2]))
            S.emit("scalar",
                   lambda e, c=cc, o=xq: e.activation(
                       wbuf[:, o:o + T], xnT[:, c * T:(c + 1) * T], AF.Square),
                   waits=w)
            sq = S.at(sE)
            for tch in range(NT):
                mm(ps1[:, 512 + tch:512 + tch + 1],
                   wbuf[:, xq + tch * 128:xq + (tch + 1) * 128],
                   sm1[:, ONESC:ONESC + 1],
                   start=(cc == 0 and tch == 0),
                   stop=(cc == NCC - 1 and tch == NT - 1),
                   waits=[(sE, sq), (vE, last_v_sb)] if tch == 0 else (),
                   inc=tE if tch == NT - 1 else None)
            bmm_done[cc] = S.at(tE)
        last_sq_sb = S.at(sE)

        chunk_free = {}
        for quad in range(NT // 4):
            par = quad % 2
            pbase = 1024 + par * 512
            for cc in range(NCC):
                for k in range(4):
                    tch = quad * 4 + k
                    lastq = (quad == NT // 4 - 1 and k == 3)
                    mm(ps1[:, pbase + k * 16:pbase + k * 16 + 9],
                       xnT[:, cc * T + tch * 128:cc * T + tch * 128 + 128],
                       sm1[:, WG9 + cc * 9:WG9 + cc * 9 + 9],
                       start=(cc == 0 and k == 0), stop=(cc == NCC - 1 and k == 3),
                       waits=[(vE, pair_evac[par])] if (cc == 0 and k == 0) else (),
                       inc=tE if (lastq or (cc == NCC - 1 and k == 3)) else None)
                    if lastq:
                        chunk_free[cc] = S.at(tE)
            own = (sb == 0)
            G4 = 793
            lgq = ps1[:, pbase:pbase + 64].rearrange(
                "p (k c) -> p k c", k=4)[:, :, 0:8]
            rawq = ps1[:, pbase:pbase + 64].rearrange(
                "p (k c) -> p k c", k=4)[:, :, 8:9]
            ssqq = ps1[:, 512 + quad * 4:512 + quad * 4 + 4]
            m4 = sm1[:, MST + quad * 4:MST + quad * 4 + 4] if own \
                else sm1[:, G4:G4 + 4]
            m2e4 = sm1[:, G4 + 4:G4 + 8]
            vpe4 = sm1[:, G4 + 8:G4 + 12]
            is4 = sm1[:, IST + quad * 4:IST + quad * 4 + 4] if own \
                else sm1[:, G4 + 12:G4 + 16]
            mis4 = sm1[:, G4 + 16:G4 + 20]
            nmx4 = sm1[:, G4 + 20:G4 + 24]
            sum4 = sm1[:, G4 + 24:G4 + 28]
            rcs4 = sm1[:, G4 + 28:G4 + 32]
            gmax4 = sm1[:, G4 + 32:G4 + 36]
            t1q = sm1[:, G4 + 36:G4 + 68]
            t1q3 = t1q.rearrange("p (k c) -> p k c", k=4)
            t2q = sm1[:, G4 + 68:G4 + 100]
            t2q3 = t2q.rearrange("p (k c) -> p k c", k=4)
            mskq = (sm1[:, MSKA + quad * 32:MSKA + quad * 32 + 32] if own
                    else t2q)

            def bc4(ap1):
                return bass.AP(ap1.tensor, ap1.offset, [ap1.ap[0], [1, 4], [0, 8]])

            def bcq(ap8):
                return bass.AP(ap8.tensor, ap8.offset, [ap8.ap[0], [0, 4], [1, 8]])

            S.emit("vector", lambda e, o=m4, i=rawq: e.tensor_scalar(
                o, i, 1.0 / D, None, ALU.mult), waits=[(tE, S.at(tE))])
            S.emit("vector", lambda e, o=m2e4, i=m4: e.tensor_mul(o, i, i))
            S.emit("vector", lambda e, o=m2e4: e.tensor_scalar(
                o, o, LN_EPS, None, ALU.subtract))
            S.emit("vector", lambda e, o=vpe4, i=ssqq: e.tensor_scalar(
                o, i, 1.0 / D, None, ALU.mult))
            S.emit("vector", lambda e, o=vpe4, i=m2e4: e.tensor_sub(o, o, i))
            vv = S.at(vE)
            S.emit("scalar", lambda e, o=is4, i=vpe4: e.activation(o, i, AF.Ln),
                   waits=[(vE, vv)])
            S.emit("scalar", lambda e, o=is4: e.activation(
                o, o, AF.Exp, scale=-0.5))
            sv = S.at(sE)
            S.emit("vector", lambda e, o=mis4, i=m4, i2=is4: e.tensor_mul(
                o, i, i2), waits=[(sE, sv)])
            S.emit("vector", lambda e, o=t1q3, i=lgq, i2=bc4(is4): e.tensor_mul(
                o, i, i2))
            S.emit("vector", lambda e, o=t2q3, i2=bc4(mis4): e.tensor_mul(
                o, bcq(sm1[:, NEGS:NEGS + 8]), i2))
            S.emit("vector", lambda e, o=t2q3, i=t1q3: e.tensor_add(o, o, i))
            S.emit("vector", lambda e, o=t2q3: e.tensor_add(
                o, o, bcq(sm1[:, BGB:BGB + 8])))
            S.emit("vector", lambda e, o=nmx4, i=t2q3: e.reduce_max(
                o, i, AX.X, negate=True))
            S.emit("vector", lambda e, o=t2q3, i2=bc4(nmx4): e.tensor_add(
                o, o, i2))
            vv = S.at(vE)
            S.emit("scalar", lambda e, o=t1q, i=t2q: e.activation(o, i, AF.Exp),
                   waits=[(vE, vv)])
            sv = S.at(sE)
            S.emit("vector", lambda e, o=sum4, i=t1q3: e.reduce_sum(o, i, AX.X),
                   waits=[(sE, sv)])
            S.emit("vector", lambda e, o=rcs4, i=sum4: e.reciprocal(o, i))
            S.emit("vector", lambda e, o=t1q3, i2=bc4(rcs4): e.tensor_mul(
                o, o, i2))
            S.emit("vector", lambda e, o=gmax4, i=t1q3: e.reduce_max(o, i, AX.X))
            S.emit("vector", lambda e, o=t2q3, i=t1q3, i2=bc4(gmax4): e.tensor_tensor(
                o, i, i2, ALU.is_equal))
            S.emit("vector", lambda e, o=mskq, i=t2q, i2=t1q: e.tensor_mul(
                o, i, i2))
            vv = S.at(vE)
            n_denom += 1
            mm(ps1[0:1, 0:32], sm1[:, ONESC:ONESC + 1], mskq,
               start=(n_denom == 1), stop=(n_denom == B * NT // 4),
               waits=[(vE, vv)], inc=tE)
            pair_evac[par] = S.at(vE)
        last_mm_sb = S.at(tE)
        last_v_sb = S.at(vE)
        prev_chunk_free = dict(chunk_free)
    t_denom = S.at(tE)

    G4 = 793
    S.emit("vector", lambda e: e.tensor_copy(
        sm1[0:1, G4:G4 + 32], ps1[0:1, 0:32]),
        waits=[(tE, t_denom)])
    S.emit("vector", lambda e: e.tensor_add(
        sm1[0:1, G4:G4 + 8], sm1[0:1, G4:G4 + 8], sm1[0:1, G4 + 8:G4 + 16]))
    S.emit("vector", lambda e: e.tensor_add(
        sm1[0:1, G4 + 16:G4 + 24], sm1[0:1, G4 + 16:G4 + 24],
        sm1[0:1, G4 + 24:G4 + 32]))
    S.emit("vector", lambda e: e.tensor_add(
        sm1[0:1, G4:G4 + 8], sm1[0:1, G4:G4 + 8], sm1[0:1, G4 + 16:G4 + 24]))
    S.emit("vector", lambda e: e.tensor_scalar(
        sm1[0:1, DENR:DENR + 8], sm1[0:1, G4:G4 + 8], GATE_EPS, None, ALU.add))
    vv = S.at(vE)
    mm(ps1[:, 0:8], sm1[0:1, ONESR:ONESR + 128], sm1[0:1, DENR:DENR + 8],
       start=True, stop=True, waits=[(vE, vv)], inc=tE)
    tv = S.at(tE)
    S.emit("vector", lambda e: e.reciprocal(sm1[:, RDB:RDB + 8], ps1[:, 0:8]),
           waits=[(tE, tv)])
    S.emit("vector", lambda e: e.tensor_scalar(
        sm1[:, RDB:RDB + 8], sm1[:, RDB:RDB + 8], float(B * T), None, ALU.mult))
    for tch in range(NT):
        t8 = sm1[:, SCR:SCR + 8]
        msk = sm1[:, MSKA + tch * 8:MSKA + tch * 8 + 8]
        S.emit("vector", lambda e, o=t8, i=msk: e.tensor_mul(
            o, i, sm1[:, RDB:RDB + 8]))
        S.emit("vector", lambda e, o=sm1[:, GATEC + tch:GATEC + tch + 1], i=t8:
               e.reduce_sum(o, i, AX.X))
        S.emit("vector", lambda e, o=t8, i=msk: e.tensor_single_scalar(
            o, i, 0.0, ALU.is_gt))
        S.emit("vector", lambda e, o=t8: e.tensor_mul(
            o, o, sm1[:, IOTA8:IOTA8 + 8]))
        S.emit("vector", lambda e, o=sm1[:, IDXC + tch:IDXC + tch + 1], i=t8:
               e.reduce_sum(o, i, AX.X))
    v_gatec = S.at(vE)

    xn_v = []
    for tch in range(NT):
        S.emit("vector",
               lambda e, t=tch: e.tensor_scalar(
                   big2[:, t * 1024:(t + 1) * 1024],
                   big2[:, t * 1024:(t + 1) * 1024],
                   sm1[:, MST + t:MST + t + 1],
                   sm1[:, IST + t:IST + t + 1],
                   ALU.subtract, ALU.mult),
               waits=[(qC, qC_all)] if tch == 0 else ())
        xn_v.append(S.at(vE))
    tr_evac = {0: v_gatec, 1: v_gatec}
    for cc in range(NCC):
        for tq in range(4):
            par = (cc * 4 + tq) % 2
            pbase = 1024 + par * 512
            for k in range(4):
                tch = tq * 4 + k
                w = [(vE, max(xn_v[tq * 4 + 3], tr_evac[par]))] if k == 0 else ()
                mm(ps1[:, pbase + k * 128:pbase + k * 128 + 128],
                   big2[:, tch * 1024 + cc * 128:tch * 1024 + cc * 128 + 128],
                   sm1[:, IDENT:IDENT + 128],
                   start=(k == 0), stop=(k == 3), is_transpose=True,
                   waits=w, inc=tE if k == 3 else None)
            tv = S.at(tE)
            S.emit("vector",
                   lambda e, c=cc, q=tq, pb=pbase: e.tensor_copy(
                       xnT[:, c * T + q * 512:c * T + q * 512 + 512],
                       ps1[:, pb:pb + 512]),
                   waits=[(tE, tv)])
            tr_evac[par] = S.at(vE)
    v_xnT = S.at(vE)
    t_xn_done = S.at(tE)

    S.emit("vector", lambda e: e.tensor_copy(
        sm1[0:1, BASER:BASER + 8], sm1[0:1, E512R:E512R + 8]),
        waits=[(qC, qC_all)])
    nsc_prev = [0, 0]
    for tch in range(NT):
        oh = sm1[:, SCR + 24:SCR + 32]
        msk = sm1[:, MSKA + tch * 8:MSKA + tch * 8 + 8]
        t8 = sm1[:, SCR:SCR + 8]
        S.emit("vector", lambda e, o=oh, i=msk: e.tensor_single_scalar(
            o, i, 0.0, ALU.is_gt),
            waits=[(vE, max(tr_evac[0], tr_evac[1]))] if tch == 0 else ())
        v_oh = S.at(vE)
        mm(ps1[:, 1024:1032], sm1[:, LTRIT:LTRIT + 128], oh,
           start=True, stop=False, waits=[(vE, v_oh)])
        mm(ps1[:, 1024:1032], sm1[0:1, ONESR:ONESR + 128],
           sm1[0:1, BASER:BASER + 8], start=False, stop=True, inc=tE)
        mm(ps1[0:1, 1536:1544], sm1[:, ONESC:ONESC + 1], oh,
           start=True, stop=True, inc=tE)
        tv = S.at(tE)
        S.emit("vector", lambda e, o=t8, i=oh: e.tensor_mul(
            o, i, ps1[:, 1024:1032]), waits=[(tE, tv)])
        S.emit("vector", lambda e, o=sm1[:, SLOTF + tch:SLOTF + tch + 1], i=t8:
               e.reduce_sum(o, i, AX.X))
        S.emit("vector", lambda e: e.tensor_add(
            sm1[0:1, BASER:BASER + 8], sm1[0:1, BASER:BASER + 8],
            ps1[0:1, 1536:1544]))
        S.emit("vector", lambda e, t=tch: e.tensor_scalar(
            sm1[:, SLOTF + t:SLOTF + t + 1], sm1[:, SLOTF + t:SLOTF + t + 1],
            float(NSLOT - 1), None, ALU.min))
        S.emit("vector", lambda e, t=tch: e.tensor_copy(
            sloti[:, t:t + 1], sm1[:, SLOTF + t:SLOTF + t + 1]))
        par = tch % 2
        w = []
        if tch >= 2:
            w.append((qS[par], nsc_prev[par]))
        S.emit("vector",
               lambda e, t=tch, p=par: e.tensor_scalar(
                   mid[:, XG_STG + p * 1024:XG_STG + (p + 1) * 1024],
                   big2[:, t * 1024:(t + 1) * 1024],
                   sm1[:, GATEC + t:GATEC + t + 1], None, ALU.mult),
               waits=w)
        v_xg = S.at(vE)
        nsc_prev[par] = S.emit(
            "gpsimd",
            lambda e, t=tch, p=par: e.indirect_dma_start(
                out=xgp[:, :],
                out_offset=IndirectOffsetOnAxis(sloti[:, t:t + 1], 0),
                in_=mid[:, XG_STG + p * 1024:XG_STG + (p + 1) * 1024],
                in_offset=None),
            waits=[(vE, v_xg)], inc=qS[par], inc_by=16)
    qS_tot = [S.at(qS[0]), S.at(qS[1])]
    v_scatter = S.at(vE)

    w1_cnt, w2_cnt = {}, {}
    w1_used, w2_used = {}, {}
    yst_cnt = {}
    pload_cnt, tr_done, yev_done = {}, {}, {}

    def emit_w1_dma(e_, s, waits):
        src = w1[e_, :, s * SET_I:(s + 1) * SET_I].rearrange(
            "(cc p) i -> p cc i", p=128)
        w1_cnt[(e_, s)] = S.emit(
            "sync",
            lambda e2, s2=src, o=(s % 2) * 2048: e2.dma_start(
                out=wbuf[:, o:o + 2048].rearrange("p (cc i) -> p cc i", cc=8),
                in_=s2),
            waits=waits, inc=qW1[s % 2], inc_by=16)

    def emit_w2_dma(e_, ic, waits):
        w2_cnt[(e_, ic)] = S.emit(
            "sync",
            lambda e2, ee=e_, i=ic: e2.dma_start(
                out=w2b[:, (i % 2) * D:(i % 2) * D + D],
                in_=w2[ee, i * 128:(i + 1) * 128, :]),
            waits=waits, inc=qW2[ic % 2], inc_by=16)

    for e_ in range(E):
        par = e_ % 2
        w = [(qS[0], qS_tot[0]), (qS[1], qS_tot[1])]
        if e_ >= 1:
            w.append((tE, tr_done[e_ - 1]))
        pload_cnt[e_] = S.emit(
            "gpsimd",
            lambda e2, ee=e_: e2.dma_start(
                out=mid[:, M_PST:M_PST + 4096].rearrange("p (t d) -> p t d", t=4),
                in_=xgp[ee * CAP:(ee + 1) * CAP, :].rearrange(
                    "(t p) d -> p t d", p=128)),
            waits=w, inc=qP[par], inc_by=16)
        for cc in range(NCC):
            pb = (cc % 2) * 512
            for t in range(4):
                w = []
                if t == 0:
                    w = [(qP[par], pload_cnt[e_])]
                    if cc < 2:
                        w.append((vE, v_scatter if e_ == 0 else yev_done[e_ - 1]))
                    else:
                        w.append((vE, S.at(vE)))
                mm(ps1[:, pb + t * 128:pb + t * 128 + 128],
                   mid[:, M_PST + t * 1024 + cc * 128:M_PST + t * 1024 + cc * 128 + 128],
                   sm1[:, IDENT:IDENT + 128],
                   start=(t == 0), stop=(t == 3), is_transpose=True,
                   waits=w, inc=tE if t == 3 else None)
            tv = S.at(tE)
            S.emit("vector",
                   lambda e2, c=cc, pb2=pb: e2.tensor_copy(
                       mid[:, M_XGT + c * 512:M_XGT + c * 512 + 512],
                       ps1[:, pb2:pb2 + 512]),
                   waits=[(tE, tv)])
        tr_done[e_] = S.at(tE)
        v_tr = S.at(vE)

        for s in (0, 1):
            if e_ == 0:
                w = [(tE, t_xn_done)] if s == 0 else ()
            else:
                w = [(tE, w1_used[(e_ - 1, NSETS - 2 + s)])]
            emit_w1_dma(e_, s, w)
        relu_ev = {}
        for ic in range(NI):
            s = ic // 2
            wofs = (s % 2) * 2048
            iofs = (ic % 2) * 128
            pb = (ic % 2) * 512
            w = []
            if ic == 0:
                w = [(qW1[0], w1_cnt[(e_, 0)]), (vE, v_tr)]
            elif ic >= 2:
                w = [(sE, relu_ev[ic - 2])]
            if ic % 2 == 0 and ic // 2 > 0:
                w.append((qW1[s % 2], w1_cnt[(e_, s)]))
            for cc in range(NCC):
                mm(ps1[:, pb:pb + 512],
                   wbuf[:, wofs + cc * SET_I + iofs:wofs + cc * SET_I + iofs + 128],
                   mid[:, M_XGT + cc * 512:M_XGT + cc * 512 + 512],
                   start=(cc == 0), stop=(cc == NCC - 1),
                   waits=w if cc == 0 else (), inc=tE if cc == NCC - 1 else None)
            if ic % 2 == 1:
                w1_used[(e_, ic // 2)] = S.at(tE)
                if ic // 2 + 2 < NSETS:
                    emit_w1_dma(e_, ic // 2 + 2, [(tE, w1_used[(e_, ic // 2)])])
            tv = S.at(tE)
            S.emit("scalar",
                   lambda e2, i=ic, p2=pb: e2.activation(
                       big2[:, i * CAP:(i + 1) * CAP], ps1[:, p2:p2 + 512], AF.Relu),
                   waits=[(tE, tv)])
            relu_ev[ic] = S.at(sE)
        s_hT = S.at(sE)

        for ic in (0, 1):
            if e_ == 0:
                w = ()
            else:
                w = [(tE, w2_used[(e_ - 1, NI - 2 + ic)])]
            emit_w2_dma(e_, ic, w)
        for ic in range(NI):
            for t in range(4):
                psy = ps0 if t < 2 else ps1
                yofs = (t % 2) * 1024
                w = []
                if t == 0:
                    w = [(qW2[ic % 2], w2_cnt[(e_, ic)]), (sE, s_hT)]
                    if ic == 0:
                        w.append((vE, yev_done.get(e_ - 1, v_scatter)))
                for dh in range(2):
                    mm(psy[:, yofs + dh * 512:yofs + dh * 512 + 512],
                       big2[:, ic * CAP + t * 128:ic * CAP + t * 128 + 128],
                       w2b[:, (ic % 2) * D + dh * 512:(ic % 2) * D + dh * 512 + 512],
                       start=(ic == 0), stop=(ic == NI - 1),
                       waits=w if dh == 0 else (),
                       inc=tE if (t == 3 and dh == 1) else None)
                    if dh == 0:
                        w = []
            w2_used[(e_, ic)] = S.at(tE)
            if ic + 2 < NI:
                emit_w2_dma(e_, ic + 2, [(tE, w2_used[(e_, ic)])])
        t_y = S.at(tE)
        for t in range(4):
            psy = ps0 if t < 2 else ps1
            yofs = (t % 2) * 1024
            gpar = (e_ * 4 + t) % 2
            w = [(tE, t_y)]
            if e_ * 4 + t >= 2:
                w.append((qY[gpar], yst_cnt[e_ * 4 + t - 2]))
            S.emit("vector",
                   lambda e2, p=psy, yo=yofs, gp=gpar: e2.tensor_copy(
                       mid[:, M_YST + gp * 1024:M_YST + (gp + 1) * 1024],
                       p[:, yo:yo + 1024]),
                   waits=w)
            vv = S.at(vE)
            yst_cnt[e_ * 4 + t] = S.emit(
                "gpsimd",
                lambda e2, ee=e_, t2=t, gp=gpar: e2.dma_start(
                    out=yperm[ee * CAP + t2 * 128:ee * CAP + (t2 + 1) * 128, :],
                    in_=mid[:, M_YST + gp * 1024:M_YST + (gp + 1) * 1024]),
                waits=[(vE, vv)], inc=qY[gpar], inc_by=16)
        yev_done[e_] = S.at(vE)
    t_ffn = S.at(tE)
    qY_tot = [S.at(qY[0]), S.at(qY[1])]

    pair_done = {}
    wq_cnt = {}
    for hp in range(H // 2):
        par = hp % 2
        wofs = 2048 * par
        for j, wsrc in enumerate((wqT, wkT)):
            w = ()
            if j == 0:
                w = [(tE, pair_done[hp - 2])] if hp >= 2 else [(tE, t_ffn)]
            src2 = wsrc.rearrange("(cc p) d -> p cc d", p=128)[
                :, :, hp * 128:(hp + 1) * 128]
            wq_cnt[(hp, j)] = S.emit(
                "sync",
                lambda e, s=src2, o=wofs + j * 1024: e.dma_start(
                    out=wbuf[:, o:o + 1024].rearrange("p (cc d) -> p cc d", cc=8),
                    in_=s),
                waits=w, inc=qWa[par], inc_by=16)
        srcv = wvT.rearrange("(cc p) d -> p cc d", p=128)[
            :, :, hp * 128:(hp + 1) * 128]
        wq_cnt[(hp, 2)] = S.emit(
            "sync",
            lambda e, s=srcv, o=par * 1024: e.dma_start(
                out=w2b[:, o:o + 1024].rearrange("p (cc d) -> p cc d", cc=8),
                in_=s),
            waits=[(tE, t_ffn)] if hp == 0 else (), inc=qWa[par], inc_by=16)

        evs = []
        for which, (dst, scale) in enumerate(((A_QT, 0.125), (A_KT, None))):
            for sp in range(4):
                pb = ((which * 4 + sp) % 2) * 512
                w = []
                if which == 0 and sp == 0:
                    w = [(qWa[par], wq_cnt[(hp, 2)]), (vE, max(v_xnT, S.at(vE))),
                         (sE, S.at(sE))]
                if len(evs) >= 2:
                    w.append((sE, evs[-2]))
                for cc in range(NCC):
                    mm(ps1[:, pb:pb + 512],
                       wbuf[:, wofs + which * 1024 + cc * 128:wofs + which * 1024 + (cc + 1) * 128],
                       xnT[:, cc * T + sp * 512:cc * T + sp * 512 + 512],
                       start=(cc == 0), stop=(cc == NCC - 1),
                       waits=w if cc == 0 else (), inc=tE if cc == NCC - 1 else None)
                tv = S.at(tE)
                if scale is None:
                    S.emit("scalar",
                           lambda e, d=dst, s=sp, p2=pb: e.copy(
                               mid[:, d + s * 512:d + s * 512 + 512],
                               ps1[:, p2:p2 + 512]),
                           waits=[(tE, tv)])
                else:
                    S.emit("scalar",
                           lambda e, d=dst, s=sp, p2=pb, sc=scale: e.mul(
                               mid[:, d + s * 512:d + s * 512 + 512],
                               ps1[:, p2:p2 + 512], sc),
                           waits=[(tE, tv)])
                evs.append(S.at(sE))
        vevs = []
        for tch in range(NT):
            pb = (tch % 2) * 512
            w = []
            if tch == 0:
                w = [(qWa[par], wq_cnt[(hp, 2)]), (sE, evs[-2])]
                if hp == 0:
                    w.append((qY[0], qY_tot[0]))
                    w.append((qY[1], qY_tot[1]))
            elif tch == 1:
                w = [(sE, evs[-1])]
            if len(vevs) >= 2:
                w.append((vE, vevs[-2]))
            for cc in range(NCC):
                mm(ps1[:, pb:pb + 128],
                   xnT[:, cc * T + tch * 128:cc * T + tch * 128 + 128],
                   w2b[:, par * 1024 + cc * 128:par * 1024 + (cc + 1) * 128],
                   start=(cc == 0), stop=(cc == NCC - 1),
                   waits=w if cc == 0 else (), inc=tE if cc == NCC - 1 else None)
            tv = S.at(tE)
            S.emit("vector",
                   lambda e, t=tch, p2=pb: e.tensor_copy(
                       mid[:, A_V + t * 128:A_V + (t + 1) * 128], ps1[:, p2:p2 + 128]),
                   waits=[(tE, tv)])
            vevs.append(S.at(vE))

        for hidx in range(2):
            h = hp * 2 + hidx
            hpar = h % 2
            w = [(tE, S.at(tE))]
            if h == 0:
                w.append((qC, qC_all))
            S.emit("gpsimd",
                   lambda e, hh=h: e.dma_start(out=bb[:, :], in_=biasm_d[hh, :, :]),
                   waits=w, inc=qA[hpar], inc_by=16)
            aug_cnt = S.at(qA[hpar])

            sE_h0 = S.at(sE)
            expdone = {}
            ptdone = {}
            NMXA, NMXB = SCR + 34, SCR + 35

            def emit_scores(qb):
                span = (qb + 1) * 128
                base = (qb % 2) * 1024 if qb <= 7 else 0
                st = 2047 - qb * 128
                nb = (span + 511) // 512
                if qb <= 1:
                    w0 = [(sE, sE_h0)]
                elif qb <= 7:
                    w0 = [(sE, expdone[qb - 2])]
                else:
                    w0 = [(sE, expdone[qb - 1])]
                w0.append((qA[hpar], aug_cnt))
                if h == 0 and qb == 0:
                    w0.append((vE, S.at(vE)))
                marks = []
                for b in range(nb):
                    n = min(512, span - b * 512)
                    last_of_chunk = (qb > 7 and b == 1) or b == nb - 1
                    mm(ps0[:, base + b * 512:base + b * 512 + n],
                       mid[hidx * 64:hidx * 64 + 64, A_QT + qb * 128:A_QT + qb * 128 + 128],
                       mid[hidx * 64:hidx * 64 + 64, A_KT + b * 512:A_KT + b * 512 + n],
                       start=True, stop=False,
                       waits=w0 if b == 0 else ())
                    mm(ps0[:, base + b * 512:base + b * 512 + n],
                       sm1[:, IDENT:IDENT + 128],
                       bb[:, st + b * 512:st + b * 512 + n],
                       start=False, stop=True,
                       inc=tE if last_of_chunk else None)
                    if last_of_chunk:
                        marks.append(S.at(tE))
                return marks

            def emit_softmax(qb, marks):
                span = (qb + 1) * 128
                base = (qb % 2) * 1024 if qb <= 7 else 0
                pdst = A_P + ((qb % 2) * 1024 if qb <= 7 else 0)
                if qb <= 7:
                    S.emit("vector",
                           lambda e, sp=span, bs=base: e.reduce_max(
                               sm1[:, NMX:NMX + 1], ps0[:, bs:bs + sp], AX.X,
                               negate=True),
                           waits=[(tE, marks[-1])])
                else:
                    S.emit("vector",
                           lambda e: e.reduce_max(
                               sm1[:, NMXA:NMXA + 1], ps0[:, 0:1024], AX.X,
                               negate=True),
                           waits=[(tE, marks[0])])
                    S.emit("vector",
                           lambda e, sp=span: e.reduce_max(
                               sm1[:, NMXB:NMXB + 1], ps0[:, 1024:sp], AX.X,
                               negate=True),
                           waits=[(tE, marks[-1])])
                    S.emit("vector",
                           lambda e: e.tensor_tensor(
                               sm1[:, NMX:NMX + 1], sm1[:, NMXA:NMXA + 1],
                               sm1[:, NMXB:NMXB + 1], ALU.min))
                vv = S.at(vE)
                w = [(vE, vv)]
                if qb >= 8:
                    w.append((tE, ptdone.get(qb - 1, 0)))
                S.emit("scalar",
                       lambda e, sp=span, bs=base, pd=pdst: e.activation(
                           mid[:, pd:pd + sp], ps0[:, bs:bs + sp], AF.Exp,
                           bias=sm1[:, NMX:NMX + 1], accum_out=sm1[:, RSUM:RSUM + 1]),
                       waits=w)
                expdone[qb] = S.at(sE)
                S.emit("vector", lambda e: e.reciprocal(sm1[:, RSC:RSC + 1],
                                                        sm1[:, RSUM:RSUM + 1]),
                       waits=[(sE, expdone[qb])])
                S.emit("vector", lambda e: e.tensor_scalar(
                    sm1[:, DRS:DRS + 128], sm1[:, IDENT:IDENT + 128],
                    sm1[:, RSC:RSC + 1], None, ALU.mult),
                    waits=[(tE, ptdone.get(qb - 1, 0))])
                return S.at(vE)

            def emit_ptpv(qb, v_drs):
                pofs = A_P + ((qb % 2) * 1024 if qb <= 7 else 0)
                G = (qb + 4) // 4
                evac_v = {}
                evac_eng = {}
                kbs_of = lambda g: list(range(g * 4, min(g * 4 + 4, qb + 1)))

                def emit_pv(g):
                    for i, kb in enumerate(kbs_of(g)):
                        mm(ps1[0:64, 1536:1664],
                           mid[:, A_V + kb * 128 + hidx * 64:A_V + kb * 128 + hidx * 64 + 64],
                           mid[:, A_PT + kb * 128:A_PT + kb * 128 + 128],
                           start=(kb == 0), stop=(kb == qb),
                           waits=[(evac_eng[g], evac_v[g])] if i == 0 else (),
                           inc=tE if kb == qb else None)

                for g in range(G):
                    ptb = 512 + (g % 2) * 512
                    w = [(vE, v_drs)] if g == 0 else []
                    if g >= 2:
                        w.append((evac_eng[g - 2], evac_v[g - 2]))
                    for i, kb in enumerate(kbs_of(g)):
                        mm(ps1[:, ptb + (kb % 4) * 128:ptb + (kb % 4) * 128 + 128],
                           mid[:, pofs + kb * 128:pofs + kb * 128 + 128],
                           sm1[:, DRS:DRS + 128],
                           start=(i == 0), stop=(i == len(kbs_of(g)) - 1),
                           waits=w if i == 0 else (),
                           inc=tE if i == len(kbs_of(g)) - 1 else None)
                    tv2 = S.at(tE)
                    eng = "vector" if g % 2 == 0 else "scalar"
                    if eng == "vector":
                        S.emit("vector",
                               lambda e, g2=g, n=len(kbs_of(g)), pb2=ptb: e.tensor_copy(
                                   mid[:, A_PT + g2 * 512:A_PT + g2 * 512 + n * 128],
                                   ps1[:, pb2:pb2 + n * 128]),
                               waits=[(tE, tv2)])
                        evac_eng[g], evac_v[g] = vE, S.at(vE)
                    else:
                        S.emit("scalar",
                               lambda e, g2=g, n=len(kbs_of(g)), pb2=ptb: e.copy(
                                   mid[:, A_PT + g2 * 512:A_PT + g2 * 512 + n * 128],
                                   ps1[:, pb2:pb2 + n * 128]),
                               waits=[(tE, tv2)])
                        evac_eng[g], evac_v[g] = sE, S.at(sE)
                    if g >= 1:
                        emit_pv(g - 1)
                emit_pv(G - 1)
                ptdone[qb] = S.at(tE)
                S.emit("vector",
                       lambda e, hh=h, q=qb: e.tensor_copy(
                           big2[(hh % 2) * 64:(hh % 2) * 64 + 64,
                                (hh // 2) * T + q * 128:(hh // 2) * T + q * 128 + 128],
                           ps1[0:64, 1536:1664]),
                       waits=[(tE, ptdone[qb])])

            drs_v = {}
            for qb in range(8):
                marks = emit_scores(qb)
                if qb >= 1:
                    emit_ptpv(qb - 1, drs_v[qb - 1])
                drs_v[qb] = emit_softmax(qb, marks)
            emit_ptpv(7, drs_v[7])
            for qb in range(8, NT):
                marks = emit_scores(qb)
                v_drs = emit_softmax(qb, marks)
                emit_ptpv(qb, v_drs)
        pair_done[hp] = S.at(tE)
    v_attn = S.at(vE)

    woTr = woT.rearrange("(cc p) d -> p cc d", p=128)
    w_wot = S.emit("sync",
                   lambda e: e.dma_start(
                       out=xnT[:, 0:8192].rearrange("p (cc d) -> p cc d", cc=8),
                       in_=woTr),
                   waits=[(tE, pair_done[H // 2 - 1])], inc=qWa[0], inc_by=16)
    ygath, xrel, ostore, comb_v = {}, {}, {}, {}
    for tch in range(NT):
        par = tch % 4
        par2 = tch % 2
        w = [(qY[0], qY_tot[0]), (qY[1], qY_tot[1]), (tE, pair_done[H // 2 - 1])]
        if tch >= 4:
            w.append((vE, comb_v[tch - 4]))
        ygath[tch] = S.emit(
            "gpsimd",
            lambda e, t=tch, p=par: e.indirect_dma_start(
                out=mid[:, p * 1024:(p + 1) * 1024],
                out_offset=None,
                in_=yperm[:, :],
                in_offset=IndirectOffsetOnAxis(sloti[:, t:t + 1], 0)),
            waits=w, inc=qR[par2], inc_by=16)
        xrel[tch] = S.emit(
            "gpsimd",
            lambda e, t=tch, p=par: e.dma_start(
                out=wbuf[:, p * 1024:(p + 1) * 1024],
                in_=x_own[t * 128:(t + 1) * 128, :]),
            inc=qR[par2], inc_by=16)
        psS = ps0 if par < 2 else ps1
        pb = (par % 2) * 1024
        w = [(qWa[0], w_wot), (vE, v_attn), (sE, S.at(sE))]
        if tch >= 4:
            w.append((vE, comb_v[tch - 4]))
        for dh in range(2):
            for cc in range(NCC):
                mm(psS[:, pb + dh * 512:pb + dh * 512 + 512],
                   big2[:, cc * T + tch * 128:cc * T + tch * 128 + 128],
                   xnT[:, cc * 1024 + dh * 512:cc * 1024 + dh * 512 + 512],
                   start=(cc == 0), stop=(cc == NCC - 1),
                   waits=w if (dh == 0 and cc == 0) else (),
                   inc=tE if cc == NCC - 1 else None)
        tv = S.at(tE)
        w = [(qR[par2], xrel[tch])]
        if tch >= 2:
            w.append((vE, comb_v[tch - 2]))
        S.emit("scalar",
               lambda e, p=par, p2=par2: e.activation(
                   mid[:, F_G + p2 * 1024:F_G + (p2 + 1) * 1024],
                   mid[:, p * 1024:(p + 1) * 1024], AF.Silu),
               waits=w)
        sv = S.at(sE)
        for dh in range(2):
            S.emit("vector",
                   lambda e, p=par, p2=par2, d=dh, ps=psS, pp=pb + dh * 512: e.tensor_mul(
                       mid[:, p * 1024 + d * 512:p * 1024 + d * 512 + 512],
                       ps[:, pp:pp + 512],
                       mid[:, F_G + p2 * 1024 + d * 512:F_G + p2 * 1024 + d * 512 + 512]),
                   waits=[(sE, sv), (tE, tv)])
        w = [(qR[par2], xrel[tch])]
        if tch >= 2:
            w.append((qO[par2], ostore[tch - 2]))
        S.emit("vector",
               lambda e, p=par, p2=par2: e.tensor_add(
                   mid[:, F_OUT + p2 * 1024:F_OUT + (p2 + 1) * 1024],
                   mid[:, p * 1024:(p + 1) * 1024],
                   wbuf[:, p * 1024:(p + 1) * 1024]),
               waits=w)
        comb_v[tch] = S.at(vE)
        ostore[tch] = S.emit(
            "gpsimd",
            lambda e, t=tch, p2=par2: e.dma_start(
                out=out_d[t * 128:(t + 1) * 128, :],
                in_=mid[:, F_OUT + p2 * 1024:F_OUT + (p2 + 1) * 1024]),
            waits=[(vE, comb_v[tch])], inc=qO[par2], inc_by=16)

    S.emit("gpsimd", lambda e: e.engine_nop(),
           waits=[(qO[0], S.at(qO[0])), (qO[1], S.at(qO[1]))])

    with nc.Block() as block:
        @block.tensor
        def _(eng):
            for op in S.ops["tensor"]:
                op(eng)

        @block.vector
        def _(eng):
            for op in S.ops["vector"]:
                op(eng)

        @block.scalar
        def _(eng):
            for op in S.ops["scalar"]:
                op(eng)

        @block.sync
        def _(eng):
            for op in S.ops["sync"]:
                op(eng)

        @block.gpsimd
        def _(eng):
            for op in S.ops["gpsimd"]:
                op(eng)

    es.close()
    return nc


def _host_inputs(L0_attn_x, Wq, Wk, Wv, Wo, Wg, bg, W1, W2):
    x = np.ascontiguousarray(np.asarray(L0_attn_x, dtype=np.float32))
    f32 = np.float32
    cst = {}
    cst["wqT"] = np.ascontiguousarray(np.asarray(Wq, f32).T)
    cst["wkT"] = np.ascontiguousarray(np.asarray(Wk, f32).T)
    cst["wvT"] = np.ascontiguousarray(np.asarray(Wv, f32).T)
    cst["woT"] = np.ascontiguousarray(np.asarray(Wo, f32).T)
    cst["wg9"] = np.ascontiguousarray(
        np.concatenate([np.asarray(Wg, f32), np.ones((D, 1), f32)], axis=1))
    cst["w1"] = np.ascontiguousarray(np.asarray(W1, f32))
    cst["w2"] = np.ascontiguousarray(np.asarray(W2, f32))
    cst["bgb"] = np.ascontiguousarray(np.tile(np.asarray(bg, f32)[None, :], (128, 1)))
    cst["ident"] = np.eye(128, dtype=f32)
    cst["ltriT"] = np.triu(np.ones((128, 128), f32), 1)
    cst["iota8"] = np.ascontiguousarray(
        np.tile(np.arange(E, dtype=f32)[None, :], (128, 1)))
    cst["onesc"] = np.ones((128, 1), f32)
    cst["onesr"] = np.ones((1, 128), f32)
    cst["e512r"] = (np.arange(E, dtype=f32) * CAP).reshape(1, E)
    slopes32 = np.asarray([2.0 ** (-8.0 * (j + 1) / H) for j in range(H)], f32)
    pp = np.arange(128)[:, None]
    ccp = np.arange(2176)[None, :]
    dist = (pp + 2047 - ccp).astype(f32)
    biasm = np.zeros((H, 128, 2176), f32)
    for h in range(H):
        bm = (-slopes32[h]) * dist
        bm[dist < 0] = NEGBIG
        biasm[h] = bm
    cst["biasm"] = biasm

    in_maps = []
    for core in range(B):
        m = dict(cst)
        m["x_own"] = np.ascontiguousarray(x[core])
        order = np.roll(np.arange(B), -core)
        m["xTf"] = np.ascontiguousarray(x[order].reshape(B * T, D).T)
        in_maps.append(m)
    return in_maps


def kernel(L0_attn_x, L0_attn_Wq, L0_attn_Wk, L0_attn_Wv, L0_attn_Wo,
           L0_ffn_Wg, L0_ffn_bg, L0_ffn_W1, L0_ffn_W2):
    in_maps = _host_inputs(L0_attn_x, L0_attn_Wq, L0_attn_Wk, L0_attn_Wv,
                           L0_attn_Wo, L0_ffn_Wg, L0_ffn_bg, L0_ffn_W1, L0_ffn_W2)
    nc = build_nc()
    res = run_bass_kernel_spmd(nc, in_maps, core_ids=list(range(B)))
    out = np.stack([np.asarray(r["out"], dtype=np.float32) for r in res.results])
    return out


if __name__ == "__main__":
    import time
    t0 = time.time()
    build_nc()
    print(f"built in {time.time() - t0:.1f}s")


# version 18 (final)
# speedup vs baseline: 1.0190x; 1.0190x over previous
import numpy as np
from contextlib import ExitStack

import concourse.bass as bass
import concourse.mybir as mybir
from concourse.bass import IndirectOffsetOnAxis
from concourse.bass_utils import run_bass_kernel_spmd

F32 = mybir.dt.float32
I32 = mybir.dt.int32
AF = mybir.ActivationFunctionType
ALU = mybir.AluOpType
AX = mybir.AxisListType

B, T, D, H, DH, E, INNER = 8, 2048, 1024, 16, 64, 8, 4096
NT = T // 128
NCC = D // 128
NI = INNER // 128
NEGBIG = -1.0e30
SET_I = 256
NSETS = INNER // SET_I
CAP = 512
NSLOT = E * CAP
LN_EPS = 1e-5
GATE_EPS = 1e-6


class Sched:
    ENGS = ("tensor", "vector", "scalar", "sync", "gpsimd")

    def __init__(self):
        self.ops = {e: [] for e in self.ENGS}
        self.cnt = {}
        self.serial = {}

    def reg(self, sem):
        self.cnt[sem] = 0
        return sem

    def emit(self, eng, fn, waits=(), inc=None, inc_by=1):
        waits = list(waits)
        ssem = self.serial.get(eng)
        if ssem is not None:
            waits.append((ssem, self.cnt[ssem]))
            if inc is None:
                inc, inc_by = ssem, 1
        waits = tuple((s, v) for (s, v) in waits if v > 0)

        def op(e):
            for (s, v) in waits:
                e.wait_ge(s, v)
            inst = fn(e)
            if inc is not None:
                inst.then_inc(inc, inc_by)

        self.ops[eng].append(op)
        if inc is not None:
            self.cnt[inc] += inc_by
        return self.cnt[inc] if inc is not None else None

    def at(self, sem):
        return self.cnt[sem]


def build_nc():
    nc = bass.Bass("TRN2", target_bir_lowering=False,
                   detect_race_conditions=False)

    x_own = nc.dram_tensor("x_own", [T, D], F32, kind="ExternalInput").ap()
    xTf = nc.dram_tensor("xTf", [D, B * T], F32, kind="ExternalInput").ap()
    wqT = nc.dram_tensor("wqT", [D, D], F32, kind="ExternalInput").ap()
    wkT = nc.dram_tensor("wkT", [D, D], F32, kind="ExternalInput").ap()
    wvT = nc.dram_tensor("wvT", [D, D], F32, kind="ExternalInput").ap()
    woT = nc.dram_tensor("woT", [D, D], F32, kind="ExternalInput").ap()
    wg9_d = nc.dram_tensor("wg9", [D, 9], F32, kind="ExternalInput").ap()
    w1 = nc.dram_tensor("w1", [E, D, INNER], F32, kind="ExternalInput").ap()
    w2 = nc.dram_tensor("w2", [E, INNER, D], F32, kind="ExternalInput").ap()
    bgb_d = nc.dram_tensor("bgb", [128, E], F32, kind="ExternalInput").ap()
    ident_d = nc.dram_tensor("ident", [128, 128], F32, kind="ExternalInput").ap()
    ltriT_d = nc.dram_tensor("ltriT", [128, 128], F32, kind="ExternalInput").ap()
    iota8_d = nc.dram_tensor("iota8", [128, E], F32, kind="ExternalInput").ap()
    onesc_d = nc.dram_tensor("onesc", [128, 1], F32, kind="ExternalInput").ap()
    onesr_d = nc.dram_tensor("onesr", [1, 128], F32, kind="ExternalInput").ap()
    e512_d = nc.dram_tensor("e512r", [1, E], F32, kind="ExternalInput").ap()
    biasm_d = nc.dram_tensor("biasm", [H, 128, 2176], F32, kind="ExternalInput").ap()
    out_d = nc.dram_tensor("out", [T, D], F32, kind="ExternalOutput").ap()
    xgp = nc.dram_tensor("xgp", [NSLOT, D], F32).ap()
    yperm = nc.dram_tensor("yperm", [NSLOT, D], F32).ap()

    S = Sched()
    es = ExitStack()
    sbt = lambda name, cols: es.enter_context(nc.sbuf_tensor(name, [128, cols], F32))

    xnT = sbt("xnT", NCC * T)
    big2 = sbt("big2", 16384)
    wbuf = sbt("wbuf", 4096)
    mid = sbt("mid", 10240)
    sm1 = sbt("sm1", 1024)
    w2b = sbt("w2b", 2 * D)
    bb = sbt("bb", 2176)
    sloti = es.enter_context(nc.sbuf_tensor("sloti", [128, NT], I32))

    IDENT, LTRIT = 0, 128
    IOTA8, BGB, NEGS, RDB, ONESC, WG9 = 256, 264, 272, 280, 288, 289
    MSKA, GATEC, IDXC, SLOTF = 361, 489, 505, 521
    SCR = 617
    ONESR = 657
    DENR = 785
    DRS = 793
    MST, IST = 921, 937
    NMX, RSUM, RSC = 953, 954, 955
    E512R, BASER = 956, 964

    M_PST, M_XGT, M_YST = 0, 4096, 8192
    A_QT, A_KT, A_P, A_PT, A_V = 0, 2048, 4096, 6144, 8192
    F_YS, F_XREL, F_OUT, F_G = 0, 2048, 4096, 6144
    XG_STG = 8192

    ps0 = es.enter_context(nc.psum_tensor("ps0", [128, 2048], F32))
    ps1 = es.enter_context(nc.psum_tensor("ps1", [128, 2048], F32))

    sem = lambda n: S.reg(es.enter_context(nc.semaphore(n)))
    qC = sem("qC")
    qX = sem("qX")
    qA = [sem("qA0"), sem("qA1")]
    qWa = [sem("qWa0"), sem("qWa1")]
    qW1 = [sem("qW1a"), sem("qW1b")]
    qW2 = [sem("qW2a"), sem("qW2b")]
    qS = [sem("qS0"), sem("qS1")]
    qP = [sem("qP0"), sem("qP1")]
    qY = [sem("qY0"), sem("qY1")]
    qR = [sem("qR0"), sem("qR1")]
    qO = [sem("qO0"), sem("qO1")]
    tE = sem("tE")
    vE = sem("vE")
    sE = sem("sE")
    S.serial = {"vector": vE, "scalar": sE}

    def mm(out, lhsT, rhs, start, stop, waits=(), inc=None, is_transpose=None):
        kw = {}
        if is_transpose:
            kw["is_transpose"] = True
        S.emit("tensor",
               lambda e, o=out, l=lhsT, r=rhs, st=start, sp=stop, k=kw: e.matmul(
                   o, l, r, start=st, stop=sp, **k),
               waits=waits, inc=inc)

    S.emit("gpsimd", lambda e: e.memset(mid[:, 0:2048], 0.0))
    for k in range(NSLOT // 256):
        S.emit("gpsimd",
               lambda e, kk=k: e.dma_start(
                   out=xgp[kk * 256:(kk + 1) * 256, :].rearrange(
                       "(t p) d -> p t d", p=128),
                   in_=mid[:, 0:2048].rearrange("p (t d) -> p t d", t=2)),
               inc=qC, inc_by=16)
    for dst, src in [
        (sm1[:, IDENT:IDENT + 128], ident_d[:, :]),
        (sm1[:, LTRIT:LTRIT + 128], ltriT_d[:, :]),
        (sm1[:, IOTA8:IOTA8 + 8], iota8_d[:, :]),
        (sm1[:, BGB:BGB + 8], bgb_d[:, :]),
        (sm1[:, ONESC:ONESC + 1], onesc_d[:, :]),
        (sm1[0:1, ONESR:ONESR + 128], onesr_d[:, :]),
        (sm1[0:1, E512R:E512R + 8], e512_d[:, :]),
        (sm1[:, WG9:WG9 + 72].rearrange("p (cc e) -> p cc e", cc=8),
         wg9_d.rearrange("(cc p) e -> p cc e", p=128)),
    ]:
        S.emit("gpsimd", lambda e, d=dst, s=src: e.dma_start(out=d, in_=s),
               inc=qC, inc_by=16)
    for tch in range(NT):
        S.emit("gpsimd",
               lambda e, t=tch: e.dma_start(
                   out=big2[:, t * 1024:(t + 1) * 1024],
                   in_=x_own[t * 128:(t + 1) * 128, :]),
               inc=qC, inc_by=16)
    qC_all = S.at(qC)

    for cc in range(NCC):
        mm(ps1[0:1, 1024:1032], sm1[:, ONESC:ONESC + 1],
           sm1[:, WG9 + cc * 9:WG9 + cc * 9 + 8],
           start=(cc == 0), stop=(cc == NCC - 1),
           waits=[(qC, qC_all)] if cc == 0 else (),
           inc=tE if cc == NCC - 1 else None)
    tv = S.at(tE)
    S.emit("vector",
           lambda e: e.tensor_scalar(sm1[0:1, DENR:DENR + 8], ps1[0:1, 1024:1032],
                                     -1.0, None, ALU.mult),
           waits=[(tE, tv)])
    vv = S.at(vE)
    mm(ps1[:, 1536:1544], sm1[0:1, ONESR:ONESR + 128], sm1[0:1, DENR:DENR + 8],
       start=True, stop=True, waits=[(vE, vv)], inc=tE)
    tv = S.at(tE)
    S.emit("vector", lambda e: e.tensor_copy(sm1[:, NEGS:NEGS + 8], ps1[:, 1536:1544]),
           waits=[(tE, tv)])
    v_negs = S.at(vE)

    n_denom = 0
    last_mm_sb = 0
    last_v_sb = 0
    last_sq_sb = 0
    prev_chunk_free = {}
    pair_evac = {0: v_negs, 1: v_negs}

    for sb in range(B):
        for cc in range(NCC):
            if sb == 0:
                w = [(qC, qC_all)] if cc == 0 else ()
            else:
                w = [(tE, prev_chunk_free[cc]), (sE, last_sq_sb)]
            S.emit(
                "gpsimd",
                lambda e, s=sb, c=cc: e.dma_start(
                    out=xnT[:, c * T:(c + 1) * T],
                    in_=xTf[c * 128:(c + 1) * 128, s * T:(s + 1) * T]),
                waits=w, inc=qX, inc_by=16)
        qx_tot = S.at(qX)
        bmm_done = {}
        for cc in range(NCC):
            xq = (cc % 2) * 2048
            w = [(qX, qx_tot)]
            if cc >= 2:
                w.append((tE, bmm_done[cc - 2]))
            S.emit("scalar",
                   lambda e, c=cc, o=xq: e.activation(
                       wbuf[:, o:o + T], xnT[:, c * T:(c + 1) * T], AF.Square),
                   waits=w)
            sq = S.at(sE)
            for tch in range(NT):
                mm(ps1[:, 512 + tch:512 + tch + 1],
                   wbuf[:, xq + tch * 128:xq + (tch + 1) * 128],
                   sm1[:, ONESC:ONESC + 1],
                   start=(cc == 0 and tch == 0),
                   stop=(cc == NCC - 1 and tch == NT - 1),
                   waits=[(sE, sq), (vE, last_v_sb)] if tch == 0 else (),
                   inc=tE if tch == NT - 1 else None)
            bmm_done[cc] = S.at(tE)
        last_sq_sb = S.at(sE)

        chunk_free = {}
        for oct_ in range(NT // 8):
            par = oct_ % 2
            pbase = 1024 + par * 512
            for cc in range(NCC):
                for k in range(8):
                    tch = oct_ * 8 + k
                    lastq = (oct_ == NT // 8 - 1 and k == 7)
                    mm(ps1[:, pbase + k * 16:pbase + k * 16 + 9],
                       xnT[:, cc * T + tch * 128:cc * T + tch * 128 + 128],
                       sm1[:, WG9 + cc * 9:WG9 + cc * 9 + 9],
                       start=(cc == 0 and k == 0), stop=(cc == NCC - 1 and k == 7),
                       waits=[(vE, pair_evac[par])] if (cc == 0 and k == 0) else (),
                       inc=tE if (lastq or (cc == NCC - 1 and k == 7)) else None)
                    if lastq:
                        chunk_free[cc] = S.at(tE)
            own = (sb == 0)
            MG = 2048
            lgq = ps1[:, pbase:pbase + 128].rearrange(
                "p (k c) -> p k c", k=8)[:, :, 0:8]
            rawq = ps1[:, pbase:pbase + 128].rearrange(
                "p (k c) -> p k c", k=8)[:, :, 8:9]
            ssqq = ps1[:, 512 + oct_ * 8:512 + oct_ * 8 + 8]
            m4 = sm1[:, MST + oct_ * 8:MST + oct_ * 8 + 8] if own \
                else mid[:, MG:MG + 8]
            m2e4 = mid[:, MG + 8:MG + 16]
            vpe4 = mid[:, MG + 16:MG + 24]
            is4 = sm1[:, IST + oct_ * 8:IST + oct_ * 8 + 8] if own \
                else mid[:, MG + 24:MG + 32]
            mis4 = mid[:, MG + 32:MG + 40]
            nmx4 = mid[:, MG + 40:MG + 48]
            sum4 = mid[:, MG + 48:MG + 56]
            rcs4 = mid[:, MG + 56:MG + 64]
            gmax4 = mid[:, MG + 64:MG + 72]
            t1q = mid[:, MG + 128:MG + 192]
            t1q3 = t1q.rearrange("p (k c) -> p k c", k=8)
            t2q = mid[:, MG + 192:MG + 256]
            t2q3 = t2q.rearrange("p (k c) -> p k c", k=8)
            mskq = (sm1[:, MSKA + oct_ * 64:MSKA + oct_ * 64 + 64] if own
                    else t2q)

            def bc4(ap1):
                return bass.AP(ap1.tensor, ap1.offset, [ap1.ap[0], [1, 8], [0, 8]])

            def bcq(ap8):
                return bass.AP(ap8.tensor, ap8.offset, [ap8.ap[0], [0, 8], [1, 8]])

            S.emit("vector", lambda e, o=m4, i=rawq: e.tensor_scalar(
                o, i, 1.0 / D, None, ALU.mult), waits=[(tE, S.at(tE))])
            S.emit("vector", lambda e, o=m2e4, i=m4: e.tensor_mul(o, i, i))
            S.emit("vector", lambda e, o=m2e4: e.tensor_scalar(
                o, o, LN_EPS, None, ALU.subtract))
            S.emit("vector", lambda e, o=vpe4, i=ssqq: e.tensor_scalar(
                o, i, 1.0 / D, None, ALU.mult))
            S.emit("vector", lambda e, o=vpe4, i=m2e4: e.tensor_sub(o, o, i))
            vv = S.at(vE)
            S.emit("scalar", lambda e, o=is4, i=vpe4: e.activation(o, i, AF.Ln),
                   waits=[(vE, vv)])
            S.emit("scalar", lambda e, o=is4: e.activation(
                o, o, AF.Exp, scale=-0.5))
            sv = S.at(sE)
            S.emit("vector", lambda e, o=mis4, i=m4, i2=is4: e.tensor_mul(
                o, i, i2), waits=[(sE, sv)])
            S.emit("vector", lambda e, o=t1q3, i=lgq, i2=bc4(is4): e.tensor_mul(
                o, i, i2))
            S.emit("vector", lambda e, o=t2q3, i2=bc4(mis4): e.tensor_mul(
                o, bcq(sm1[:, NEGS:NEGS + 8]), i2))
            S.emit("vector", lambda e, o=t2q3, i=t1q3: e.tensor_add(o, o, i))
            S.emit("vector", lambda e, o=t2q3: e.tensor_add(
                o, o, bcq(sm1[:, BGB:BGB + 8])))
            S.emit("vector", lambda e, o=nmx4, i=t2q3: e.reduce_max(
                o, i, AX.X, negate=True))
            S.emit("vector", lambda e, o=t2q3, i2=bc4(nmx4): e.tensor_add(
                o, o, i2))
            vv = S.at(vE)
            S.emit("scalar", lambda e, o=t1q, i=t2q: e.activation(o, i, AF.Exp),
                   waits=[(vE, vv)])
            sv = S.at(sE)
            S.emit("vector", lambda e, o=sum4, i=t1q3: e.reduce_sum(o, i, AX.X),
                   waits=[(sE, sv)])
            S.emit("vector", lambda e, o=rcs4, i=sum4: e.reciprocal(o, i))
            S.emit("vector", lambda e, o=t1q3, i2=bc4(rcs4): e.tensor_mul(
                o, o, i2))
            S.emit("vector", lambda e, o=gmax4, i=t1q3: e.reduce_max(o, i, AX.X))
            S.emit("vector", lambda e, o=t2q3, i=t1q3, i2=bc4(gmax4): e.tensor_tensor(
                o, i, i2, ALU.is_equal))
            S.emit("vector", lambda e, o=mskq, i=t2q, i2=t1q: e.tensor_mul(
                o, i, i2))
            vv = S.at(vE)
            n_denom += 1
            mm(ps1[0:1, 0:64], sm1[:, ONESC:ONESC + 1], mskq,
               start=(n_denom == 1), stop=(n_denom == B * NT // 8),
               waits=[(vE, vv)], inc=tE)
            pair_evac[par] = S.at(vE)
        last_mm_sb = S.at(tE)
        last_v_sb = S.at(vE)
        prev_chunk_free = dict(chunk_free)
    t_denom = S.at(tE)

    G4 = 793
    S.emit("vector", lambda e: e.tensor_copy(
        mid[0:1, 2304:2368], ps1[0:1, 0:64]),
        waits=[(tE, t_denom)])
    S.emit("vector", lambda e: e.tensor_add(
        mid[0:1, 2304:2336], mid[0:1, 2304:2336], mid[0:1, 2336:2368]))
    S.emit("vector", lambda e: e.tensor_add(
        mid[0:1, 2304:2320], mid[0:1, 2304:2320], mid[0:1, 2320:2336]))
    S.emit("vector", lambda e: e.tensor_add(
        mid[0:1, 2304:2312], mid[0:1, 2304:2312], mid[0:1, 2312:2320]))
    S.emit("vector", lambda e: e.tensor_scalar(
        sm1[0:1, DENR:DENR + 8], mid[0:1, 2304:2312], GATE_EPS, None, ALU.add))
    vv = S.at(vE)
    mm(ps1[:, 0:8], sm1[0:1, ONESR:ONESR + 128], sm1[0:1, DENR:DENR + 8],
       start=True, stop=True, waits=[(vE, vv)], inc=tE)
    tv = S.at(tE)
    S.emit("vector", lambda e: e.reciprocal(sm1[:, RDB:RDB + 8], ps1[:, 0:8]),
           waits=[(tE, tv)])
    S.emit("vector", lambda e: e.tensor_scalar(
        sm1[:, RDB:RDB + 8], sm1[:, RDB:RDB + 8], float(B * T), None, ALU.mult))
    for tch in range(NT):
        t8 = sm1[:, SCR:SCR + 8]
        msk = sm1[:, MSKA + tch * 8:MSKA + tch * 8 + 8]
        S.emit("vector", lambda e, o=t8, i=msk: e.tensor_mul(
            o, i, sm1[:, RDB:RDB + 8]))
        S.emit("vector", lambda e, o=sm1[:, GATEC + tch:GATEC + tch + 1], i=t8:
               e.reduce_sum(o, i, AX.X))
        S.emit("vector", lambda e, o=t8, i=msk: e.tensor_single_scalar(
            o, i, 0.0, ALU.is_gt))
        S.emit("vector", lambda e, o=t8: e.tensor_mul(
            o, o, sm1[:, IOTA8:IOTA8 + 8]))
        S.emit("vector", lambda e, o=sm1[:, IDXC + tch:IDXC + tch + 1], i=t8:
               e.reduce_sum(o, i, AX.X))
    v_gatec = S.at(vE)

    xn_v = []
    for tch in range(NT):
        S.emit("vector",
               lambda e, t=tch: e.tensor_scalar(
                   big2[:, t * 1024:(t + 1) * 1024],
                   big2[:, t * 1024:(t + 1) * 1024],
                   sm1[:, MST + t:MST + t + 1],
                   sm1[:, IST + t:IST + t + 1],
                   ALU.subtract, ALU.mult),
               waits=[(qC, qC_all)] if tch == 0 else ())
        xn_v.append(S.at(vE))
    tr_evac = {0: v_gatec, 1: v_gatec}
    for cc in range(NCC):
        for tq in range(4):
            par = (cc * 4 + tq) % 2
            pbase = 1024 + par * 512
            for k in range(4):
                tch = tq * 4 + k
                w = [(vE, max(xn_v[tq * 4 + 3], tr_evac[par]))] if k == 0 else ()
                mm(ps1[:, pbase + k * 128:pbase + k * 128 + 128],
                   big2[:, tch * 1024 + cc * 128:tch * 1024 + cc * 128 + 128],
                   sm1[:, IDENT:IDENT + 128],
                   start=(k == 0), stop=(k == 3), is_transpose=True,
                   waits=w, inc=tE if k == 3 else None)
            tv = S.at(tE)
            S.emit("vector",
                   lambda e, c=cc, q=tq, pb=pbase: e.tensor_copy(
                       xnT[:, c * T + q * 512:c * T + q * 512 + 512],
                       ps1[:, pb:pb + 512]),
                   waits=[(tE, tv)])
            tr_evac[par] = S.at(vE)
    v_xnT = S.at(vE)
    t_xn_done = S.at(tE)

    S.emit("vector", lambda e: e.tensor_copy(
        sm1[0:1, BASER:BASER + 8], sm1[0:1, E512R:E512R + 8]),
        waits=[(qC, qC_all)])
    nsc_prev = [0, 0]
    for tch in range(NT):
        oh = sm1[:, SCR + 24:SCR + 32]
        msk = sm1[:, MSKA + tch * 8:MSKA + tch * 8 + 8]
        t8 = sm1[:, SCR:SCR + 8]
        S.emit("vector", lambda e, o=oh, i=msk: e.tensor_single_scalar(
            o, i, 0.0, ALU.is_gt),
            waits=[(vE, max(tr_evac[0], tr_evac[1]))] if tch == 0 else ())
        v_oh = S.at(vE)
        mm(ps1[:, 1024:1032], sm1[:, LTRIT:LTRIT + 128], oh,
           start=True, stop=False, waits=[(vE, v_oh)])
        mm(ps1[:, 1024:1032], sm1[0:1, ONESR:ONESR + 128],
           sm1[0:1, BASER:BASER + 8], start=False, stop=True, inc=tE)
        mm(ps1[0:1, 1536:1544], sm1[:, ONESC:ONESC + 1], oh,
           start=True, stop=True, inc=tE)
        tv = S.at(tE)
        S.emit("vector", lambda e, o=t8, i=oh: e.tensor_mul(
            o, i, ps1[:, 1024:1032]), waits=[(tE, tv)])
        S.emit("vector", lambda e, o=sm1[:, SLOTF + tch:SLOTF + tch + 1], i=t8:
               e.reduce_sum(o, i, AX.X))
        S.emit("vector", lambda e: e.tensor_add(
            sm1[0:1, BASER:BASER + 8], sm1[0:1, BASER:BASER + 8],
            ps1[0:1, 1536:1544]))
        S.emit("vector", lambda e, t=tch: e.tensor_scalar(
            sm1[:, SLOTF + t:SLOTF + t + 1], sm1[:, SLOTF + t:SLOTF + t + 1],
            float(NSLOT - 1), None, ALU.min))
        S.emit("vector", lambda e, t=tch: e.tensor_copy(
            sloti[:, t:t + 1], sm1[:, SLOTF + t:SLOTF + t + 1]))
        par = tch % 2
        w = []
        if tch >= 2:
            w.append((qS[par], nsc_prev[par]))
        S.emit("vector",
               lambda e, t=tch, p=par: e.tensor_scalar(
                   mid[:, XG_STG + p * 1024:XG_STG + (p + 1) * 1024],
                   big2[:, t * 1024:(t + 1) * 1024],
                   sm1[:, GATEC + t:GATEC + t + 1], None, ALU.mult),
               waits=w)
        v_xg = S.at(vE)
        nsc_prev[par] = S.emit(
            "gpsimd",
            lambda e, t=tch, p=par: e.indirect_dma_start(
                out=xgp[:, :],
                out_offset=IndirectOffsetOnAxis(sloti[:, t:t + 1], 0),
                in_=mid[:, XG_STG + p * 1024:XG_STG + (p + 1) * 1024],
                in_offset=None),
            waits=[(vE, v_xg)], inc=qS[par], inc_by=16)
    qS_tot = [S.at(qS[0]), S.at(qS[1])]
    v_scatter = S.at(vE)

    w1_cnt, w2_cnt = {}, {}
    w1_used, w2_used = {}, {}
    yst_cnt = {}
    pload_cnt, tr_done, yev_done = {}, {}, {}

    def emit_w1_dma(e_, s, waits):
        src = w1[e_, :, s * SET_I:(s + 1) * SET_I].rearrange(
            "(cc p) i -> p cc i", p=128)
        w1_cnt[(e_, s)] = S.emit(
            "sync",
            lambda e2, s2=src, o=(s % 2) * 2048: e2.dma_start(
                out=wbuf[:, o:o + 2048].rearrange("p (cc i) -> p cc i", cc=8),
                in_=s2),
            waits=waits, inc=qW1[s % 2], inc_by=16)

    def emit_w2_dma(e_, ic, waits):
        w2_cnt[(e_, ic)] = S.emit(
            "sync",
            lambda e2, ee=e_, i=ic: e2.dma_start(
                out=w2b[:, (i % 2) * D:(i % 2) * D + D],
                in_=w2[ee, i * 128:(i + 1) * 128, :]),
            waits=waits, inc=qW2[ic % 2], inc_by=16)

    for e_ in range(E):
        par = e_ % 2
        w = [(qS[0], qS_tot[0]), (qS[1], qS_tot[1])]
        if e_ >= 1:
            w.append((tE, tr_done[e_ - 1]))
        pload_cnt[e_] = S.emit(
            "gpsimd",
            lambda e2, ee=e_: e2.dma_start(
                out=mid[:, M_PST:M_PST + 4096].rearrange("p (t d) -> p t d", t=4),
                in_=xgp[ee * CAP:(ee + 1) * CAP, :].rearrange(
                    "(t p) d -> p t d", p=128)),
            waits=w, inc=qP[par], inc_by=16)
        for cc in range(NCC):
            pb = (cc % 2) * 512
            for t in range(4):
                w = []
                if t == 0:
                    w = [(qP[par], pload_cnt[e_])]
                    if cc < 2:
                        w.append((vE, v_scatter if e_ == 0 else yev_done[e_ - 1]))
                    else:
                        w.append((vE, S.at(vE)))
                mm(ps1[:, pb + t * 128:pb + t * 128 + 128],
                   mid[:, M_PST + t * 1024 + cc * 128:M_PST + t * 1024 + cc * 128 + 128],
                   sm1[:, IDENT:IDENT + 128],
                   start=(t == 0), stop=(t == 3), is_transpose=True,
                   waits=w, inc=tE if t == 3 else None)
            tv = S.at(tE)
            S.emit("vector",
                   lambda e2, c=cc, pb2=pb: e2.tensor_copy(
                       mid[:, M_XGT + c * 512:M_XGT + c * 512 + 512],
                       ps1[:, pb2:pb2 + 512]),
                   waits=[(tE, tv)])
        tr_done[e_] = S.at(tE)
        v_tr = S.at(vE)

        for s in (0, 1):
            if e_ == 0:
                w = [(tE, t_xn_done)] if s == 0 else ()
            else:
                w = [(tE, w1_used[(e_ - 1, NSETS - 2 + s)])]
            emit_w1_dma(e_, s, w)
        relu_ev = {}
        for ic in range(NI):
            s = ic // 2
            wofs = (s % 2) * 2048
            iofs = (ic % 2) * 128
            pb = (ic % 2) * 512
            w = []
            if ic == 0:
                w = [(qW1[0], w1_cnt[(e_, 0)]), (vE, v_tr)]
            elif ic >= 2:
                w = [(sE, relu_ev[ic - 2])]
            if ic % 2 == 0 and ic // 2 > 0:
                w.append((qW1[s % 2], w1_cnt[(e_, s)]))
            for cc in range(NCC):
                mm(ps1[:, pb:pb + 512],
                   wbuf[:, wofs + cc * SET_I + iofs:wofs + cc * SET_I + iofs + 128],
                   mid[:, M_XGT + cc * 512:M_XGT + cc * 512 + 512],
                   start=(cc == 0), stop=(cc == NCC - 1),
                   waits=w if cc == 0 else (), inc=tE if cc == NCC - 1 else None)
            if ic % 2 == 1:
                w1_used[(e_, ic // 2)] = S.at(tE)
                if ic // 2 + 2 < NSETS:
                    emit_w1_dma(e_, ic // 2 + 2, [(tE, w1_used[(e_, ic // 2)])])
            tv = S.at(tE)
            S.emit("scalar",
                   lambda e2, i=ic, p2=pb: e2.activation(
                       big2[:, i * CAP:(i + 1) * CAP], ps1[:, p2:p2 + 512], AF.Relu),
                   waits=[(tE, tv)])
            relu_ev[ic] = S.at(sE)
        s_hT = S.at(sE)

        for ic in (0, 1):
            if e_ == 0:
                w = ()
            else:
                w = [(tE, w2_used[(e_ - 1, NI - 2 + ic)])]
            emit_w2_dma(e_, ic, w)
        for ic in range(NI):
            for t in range(4):
                psy = ps0 if t < 2 else ps1
                yofs = (t % 2) * 1024
                w = []
                if t == 0:
                    w = [(qW2[ic % 2], w2_cnt[(e_, ic)]), (sE, s_hT)]
                    if ic == 0:
                        w.append((vE, yev_done.get(e_ - 1, v_scatter)))
                for dh in range(2):
                    mm(psy[:, yofs + dh * 512:yofs + dh * 512 + 512],
                       big2[:, ic * CAP + t * 128:ic * CAP + t * 128 + 128],
                       w2b[:, (ic % 2) * D + dh * 512:(ic % 2) * D + dh * 512 + 512],
                       start=(ic == 0), stop=(ic == NI - 1),
                       waits=w if dh == 0 else (),
                       inc=tE if (t == 3 and dh == 1) else None)
                    if dh == 0:
                        w = []
            w2_used[(e_, ic)] = S.at(tE)
            if ic + 2 < NI:
                emit_w2_dma(e_, ic + 2, [(tE, w2_used[(e_, ic)])])
        t_y = S.at(tE)
        for t in range(4):
            psy = ps0 if t < 2 else ps1
            yofs = (t % 2) * 1024
            gpar = (e_ * 4 + t) % 2
            w = [(tE, t_y)]
            if e_ * 4 + t >= 2:
                w.append((qY[gpar], yst_cnt[e_ * 4 + t - 2]))
            S.emit("vector",
                   lambda e2, p=psy, yo=yofs, gp=gpar: e2.tensor_copy(
                       mid[:, M_YST + gp * 1024:M_YST + (gp + 1) * 1024],
                       p[:, yo:yo + 1024]),
                   waits=w)
            vv = S.at(vE)
            yst_cnt[e_ * 4 + t] = S.emit(
                "gpsimd",
                lambda e2, ee=e_, t2=t, gp=gpar: e2.dma_start(
                    out=yperm[ee * CAP + t2 * 128:ee * CAP + (t2 + 1) * 128, :],
                    in_=mid[:, M_YST + gp * 1024:M_YST + (gp + 1) * 1024]),
                waits=[(vE, vv)], inc=qY[gpar], inc_by=16)
        yev_done[e_] = S.at(vE)
    t_ffn = S.at(tE)
    qY_tot = [S.at(qY[0]), S.at(qY[1])]

    pair_done = {}
    wq_cnt = {}
    for hp in range(H // 2):
        par = hp % 2
        wofs = 2048 * par
        for j, wsrc in enumerate((wqT, wkT)):
            w = ()
            if j == 0:
                w = [(tE, pair_done[hp - 2])] if hp >= 2 else [(tE, t_ffn)]
            src2 = wsrc.rearrange("(cc p) d -> p cc d", p=128)[
                :, :, hp * 128:(hp + 1) * 128]
            wq_cnt[(hp, j)] = S.emit(
                "sync",
                lambda e, s=src2, o=wofs + j * 1024: e.dma_start(
                    out=wbuf[:, o:o + 1024].rearrange("p (cc d) -> p cc d", cc=8),
                    in_=s),
                waits=w, inc=qWa[par], inc_by=16)
        srcv = wvT.rearrange("(cc p) d -> p cc d", p=128)[
            :, :, hp * 128:(hp + 1) * 128]
        wq_cnt[(hp, 2)] = S.emit(
            "sync",
            lambda e, s=srcv, o=par * 1024: e.dma_start(
                out=w2b[:, o:o + 1024].rearrange("p (cc d) -> p cc d", cc=8),
                in_=s),
            waits=[(tE, t_ffn)] if hp == 0 else (), inc=qWa[par], inc_by=16)

        evs = []
        for which, (dst, scale) in enumerate(((A_QT, 0.125), (A_KT, None))):
            for sp in range(4):
                pb = ((which * 4 + sp) % 2) * 512
                w = []
                if which == 0 and sp == 0:
                    w = [(qWa[par], wq_cnt[(hp, 2)]), (vE, max(v_xnT, S.at(vE))),
                         (sE, S.at(sE))]
                if len(evs) >= 2:
                    w.append((sE, evs[-2]))
                for cc in range(NCC):
                    mm(ps1[:, pb:pb + 512],
                       wbuf[:, wofs + which * 1024 + cc * 128:wofs + which * 1024 + (cc + 1) * 128],
                       xnT[:, cc * T + sp * 512:cc * T + sp * 512 + 512],
                       start=(cc == 0), stop=(cc == NCC - 1),
                       waits=w if cc == 0 else (), inc=tE if cc == NCC - 1 else None)
                tv = S.at(tE)
                if scale is None:
                    S.emit("scalar",
                           lambda e, d=dst, s=sp, p2=pb: e.copy(
                               mid[:, d + s * 512:d + s * 512 + 512],
                               ps1[:, p2:p2 + 512]),
                           waits=[(tE, tv)])
                else:
                    S.emit("scalar",
                           lambda e, d=dst, s=sp, p2=pb, sc=scale: e.mul(
                               mid[:, d + s * 512:d + s * 512 + 512],
                               ps1[:, p2:p2 + 512], sc),
                           waits=[(tE, tv)])
                evs.append(S.at(sE))
        vevs = []
        for tch in range(NT):
            pb = (tch % 2) * 512
            w = []
            if tch == 0:
                w = [(qWa[par], wq_cnt[(hp, 2)]), (sE, evs[-2])]
                if hp == 0:
                    w.append((qY[0], qY_tot[0]))
                    w.append((qY[1], qY_tot[1]))
            elif tch == 1:
                w = [(sE, evs[-1])]
            if len(vevs) >= 2:
                w.append((vE, vevs[-2]))
            for cc in range(NCC):
                mm(ps1[:, pb:pb + 128],
                   xnT[:, cc * T + tch * 128:cc * T + tch * 128 + 128],
                   w2b[:, par * 1024 + cc * 128:par * 1024 + (cc + 1) * 128],
                   start=(cc == 0), stop=(cc == NCC - 1),
                   waits=w if cc == 0 else (), inc=tE if cc == NCC - 1 else None)
            tv = S.at(tE)
            S.emit("vector",
                   lambda e, t=tch, p2=pb: e.tensor_copy(
                       mid[:, A_V + t * 128:A_V + (t + 1) * 128], ps1[:, p2:p2 + 128]),
                   waits=[(tE, tv)])
            vevs.append(S.at(vE))

        for hidx in range(2):
            h = hp * 2 + hidx
            hpar = h % 2
            w = [(tE, S.at(tE))]
            if h == 0:
                w.append((qC, qC_all))
            S.emit("gpsimd",
                   lambda e, hh=h: e.dma_start(out=bb[:, :], in_=biasm_d[hh, :, :]),
                   waits=w, inc=qA[hpar], inc_by=16)
            aug_cnt = S.at(qA[hpar])

            sE_h0 = S.at(sE)
            expdone = {}
            ptdone = {}
            NMXA, NMXB = SCR + 34, SCR + 35

            def emit_scores(qb):
                span = (qb + 1) * 128
                base = (qb % 2) * 1024 if qb <= 7 else 0
                st = 2047 - qb * 128
                nb = (span + 511) // 512
                if qb <= 1:
                    w0 = [(sE, sE_h0)]
                elif qb <= 7:
                    w0 = [(sE, expdone[qb - 2])]
                else:
                    w0 = [(sE, expdone[qb - 1])]
                w0.append((qA[hpar], aug_cnt))
                if h == 0 and qb == 0:
                    w0.append((vE, S.at(vE)))
                marks = []
                for b in range(nb):
                    n = min(512, span - b * 512)
                    last_of_chunk = (qb > 7 and b == 1) or b == nb - 1
                    mm(ps0[:, base + b * 512:base + b * 512 + n],
                       mid[hidx * 64:hidx * 64 + 64, A_QT + qb * 128:A_QT + qb * 128 + 128],
                       mid[hidx * 64:hidx * 64 + 64, A_KT + b * 512:A_KT + b * 512 + n],
                       start=True, stop=False,
                       waits=w0 if b == 0 else ())
                    mm(ps0[:, base + b * 512:base + b * 512 + n],
                       sm1[:, IDENT:IDENT + 128],
                       bb[:, st + b * 512:st + b * 512 + n],
                       start=False, stop=True,
                       inc=tE if last_of_chunk else None)
                    if last_of_chunk:
                        marks.append(S.at(tE))
                return marks

            def emit_softmax(qb, marks):
                span = (qb + 1) * 128
                base = (qb % 2) * 1024 if qb <= 7 else 0
                pdst = A_P + ((qb % 2) * 1024 if qb <= 7 else 0)
                if qb <= 7:
                    S.emit("vector",
                           lambda e, sp=span, bs=base: e.reduce_max(
                               sm1[:, NMX:NMX + 1], ps0[:, bs:bs + sp], AX.X,
                               negate=True),
                           waits=[(tE, marks[-1])])
                else:
                    S.emit("vector",
                           lambda e: e.reduce_max(
                               sm1[:, NMXA:NMXA + 1], ps0[:, 0:1024], AX.X,
                               negate=True),
                           waits=[(tE, marks[0])])
                    S.emit("vector",
                           lambda e, sp=span: e.reduce_max(
                               sm1[:, NMXB:NMXB + 1], ps0[:, 1024:sp], AX.X,
                               negate=True),
                           waits=[(tE, marks[-1])])
                    S.emit("vector",
                           lambda e: e.tensor_tensor(
                               sm1[:, NMX:NMX + 1], sm1[:, NMXA:NMXA + 1],
                               sm1[:, NMXB:NMXB + 1], ALU.min))
                vv = S.at(vE)
                w = [(vE, vv)]
                if qb >= 8:
                    w.append((tE, ptdone.get(qb - 1, 0)))
                S.emit("scalar",
                       lambda e, sp=span, bs=base, pd=pdst: e.activation(
                           mid[:, pd:pd + sp], ps0[:, bs:bs + sp], AF.Exp,
                           bias=sm1[:, NMX:NMX + 1], accum_out=sm1[:, RSUM:RSUM + 1]),
                       waits=w)
                expdone[qb] = S.at(sE)
                S.emit("vector", lambda e: e.reciprocal(sm1[:, RSC:RSC + 1],
                                                        sm1[:, RSUM:RSUM + 1]),
                       waits=[(sE, expdone[qb])])
                S.emit("vector", lambda e: e.tensor_scalar(
                    sm1[:, DRS:DRS + 128], sm1[:, IDENT:IDENT + 128],
                    sm1[:, RSC:RSC + 1], None, ALU.mult),
                    waits=[(tE, ptdone.get(qb - 1, 0))])
                return S.at(vE)

            def emit_ptpv(qb, v_drs):
                pofs = A_P + ((qb % 2) * 1024 if qb <= 7 else 0)
                G = (qb + 4) // 4
                evac_v = {}
                kbs_of = lambda g: list(range(g * 4, min(g * 4 + 4, qb + 1)))

                def emit_pv(g):
                    for i, kb in enumerate(kbs_of(g)):
                        mm(ps1[0:64, 1536:1664],
                           mid[:, A_V + kb * 128 + hidx * 64:A_V + kb * 128 + hidx * 64 + 64],
                           mid[:, A_PT + kb * 128:A_PT + kb * 128 + 128],
                           start=(kb == 0), stop=(kb == qb),
                           waits=[(vE, evac_v[g])] if i == 0 else (),
                           inc=tE if kb == qb else None)

                for g in range(G):
                    ptb = 512 + (g % 2) * 512
                    w = [(vE, v_drs)] if g == 0 else []
                    if g >= 2:
                        w.append((vE, evac_v[g - 2]))
                    for i, kb in enumerate(kbs_of(g)):
                        mm(ps1[:, ptb + (kb % 4) * 128:ptb + (kb % 4) * 128 + 128],
                           mid[:, pofs + kb * 128:pofs + kb * 128 + 128],
                           sm1[:, DRS:DRS + 128],
                           start=(i == 0), stop=(i == len(kbs_of(g)) - 1),
                           waits=w if i == 0 else (),
                           inc=tE if i == len(kbs_of(g)) - 1 else None)
                    tv2 = S.at(tE)
                    S.emit("vector",
                           lambda e, g2=g, n=len(kbs_of(g)), pb2=ptb: e.tensor_copy(
                               mid[:, A_PT + g2 * 512:A_PT + g2 * 512 + n * 128],
                               ps1[:, pb2:pb2 + n * 128]),
                           waits=[(tE, tv2)])
                    evac_v[g] = S.at(vE)
                    if g >= 1:
                        emit_pv(g - 1)
                emit_pv(G - 1)
                ptdone[qb] = S.at(tE)
                S.emit("vector",
                       lambda e, hh=h, q=qb: e.tensor_copy(
                           big2[(hh % 2) * 64:(hh % 2) * 64 + 64,
                                (hh // 2) * T + q * 128:(hh // 2) * T + q * 128 + 128],
                           ps1[0:64, 1536:1664]),
                       waits=[(tE, ptdone[qb])])

            drs_v = {}
            for qb in range(8):
                marks = emit_scores(qb)
                if qb >= 1:
                    emit_ptpv(qb - 1, drs_v[qb - 1])
                drs_v[qb] = emit_softmax(qb, marks)
            emit_ptpv(7, drs_v[7])
            for qb in range(8, NT):
                marks = emit_scores(qb)
                v_drs = emit_softmax(qb, marks)
                emit_ptpv(qb, v_drs)
        pair_done[hp] = S.at(tE)
    v_attn = S.at(vE)

    woTr = woT.rearrange("(cc p) d -> p cc d", p=128)
    w_wot = S.emit("sync",
                   lambda e: e.dma_start(
                       out=xnT[:, 0:8192].rearrange("p (cc d) -> p cc d", cc=8),
                       in_=woTr),
                   waits=[(tE, pair_done[H // 2 - 1])], inc=qWa[0], inc_by=16)
    ygath, xrel, ostore, comb_v = {}, {}, {}, {}
    for tch in range(NT):
        par = tch % 4
        par2 = tch % 2
        w = [(qY[0], qY_tot[0]), (qY[1], qY_tot[1]), (tE, pair_done[H // 2 - 1])]
        if tch >= 4:
            w.append((vE, comb_v[tch - 4]))
        ygath[tch] = S.emit(
            "gpsimd",
            lambda e, t=tch, p=par: e.indirect_dma_start(
                out=mid[:, p * 1024:(p + 1) * 1024],
                out_offset=None,
                in_=yperm[:, :],
                in_offset=IndirectOffsetOnAxis(sloti[:, t:t + 1], 0)),
            waits=w, inc=qR[par2], inc_by=16)
        xrel[tch] = S.emit(
            "gpsimd",
            lambda e, t=tch, p=par: e.dma_start(
                out=wbuf[:, p * 1024:(p + 1) * 1024],
                in_=x_own[t * 128:(t + 1) * 128, :]),
            inc=qR[par2], inc_by=16)
        psS = ps0 if par < 2 else ps1
        pb = (par % 2) * 1024
        w = [(qWa[0], w_wot), (vE, v_attn), (sE, S.at(sE))]
        if tch >= 4:
            w.append((vE, comb_v[tch - 4]))
        for dh in range(2):
            for cc in range(NCC):
                mm(psS[:, pb + dh * 512:pb + dh * 512 + 512],
                   big2[:, cc * T + tch * 128:cc * T + tch * 128 + 128],
                   xnT[:, cc * 1024 + dh * 512:cc * 1024 + dh * 512 + 512],
                   start=(cc == 0), stop=(cc == NCC - 1),
                   waits=w if (dh == 0 and cc == 0) else (),
                   inc=tE if cc == NCC - 1 else None)
        tv = S.at(tE)
        w = [(qR[par2], xrel[tch])]
        if tch >= 2:
            w.append((vE, comb_v[tch - 2]))
        S.emit("scalar",
               lambda e, p=par, p2=par2: e.activation(
                   mid[:, F_G + p2 * 1024:F_G + (p2 + 1) * 1024],
                   mid[:, p * 1024:(p + 1) * 1024], AF.Silu),
               waits=w)
        sv = S.at(sE)
        for dh in range(2):
            S.emit("vector",
                   lambda e, p=par, p2=par2, d=dh, ps=psS, pp=pb + dh * 512: e.tensor_mul(
                       mid[:, p * 1024 + d * 512:p * 1024 + d * 512 + 512],
                       ps[:, pp:pp + 512],
                       mid[:, F_G + p2 * 1024 + d * 512:F_G + p2 * 1024 + d * 512 + 512]),
                   waits=[(sE, sv), (tE, tv)])
        w = [(qR[par2], xrel[tch])]
        if tch >= 2:
            w.append((qO[par2], ostore[tch - 2]))
        S.emit("vector",
               lambda e, p=par, p2=par2: e.tensor_add(
                   mid[:, F_OUT + p2 * 1024:F_OUT + (p2 + 1) * 1024],
                   mid[:, p * 1024:(p + 1) * 1024],
                   wbuf[:, p * 1024:(p + 1) * 1024]),
               waits=w)
        comb_v[tch] = S.at(vE)
        ostore[tch] = S.emit(
            "gpsimd",
            lambda e, t=tch, p2=par2: e.dma_start(
                out=out_d[t * 128:(t + 1) * 128, :],
                in_=mid[:, F_OUT + p2 * 1024:F_OUT + (p2 + 1) * 1024]),
            waits=[(vE, comb_v[tch])], inc=qO[par2], inc_by=16)

    S.emit("gpsimd", lambda e: e.engine_nop(),
           waits=[(qO[0], S.at(qO[0])), (qO[1], S.at(qO[1]))])

    with nc.Block() as block:
        @block.tensor
        def _(eng):
            for op in S.ops["tensor"]:
                op(eng)

        @block.vector
        def _(eng):
            for op in S.ops["vector"]:
                op(eng)

        @block.scalar
        def _(eng):
            for op in S.ops["scalar"]:
                op(eng)

        @block.sync
        def _(eng):
            for op in S.ops["sync"]:
                op(eng)

        @block.gpsimd
        def _(eng):
            for op in S.ops["gpsimd"]:
                op(eng)

    es.close()
    return nc


def _host_inputs(L0_attn_x, Wq, Wk, Wv, Wo, Wg, bg, W1, W2):
    x = np.ascontiguousarray(np.asarray(L0_attn_x, dtype=np.float32))
    f32 = np.float32
    cst = {}
    cst["wqT"] = np.ascontiguousarray(np.asarray(Wq, f32).T)
    cst["wkT"] = np.ascontiguousarray(np.asarray(Wk, f32).T)
    cst["wvT"] = np.ascontiguousarray(np.asarray(Wv, f32).T)
    cst["woT"] = np.ascontiguousarray(np.asarray(Wo, f32).T)
    cst["wg9"] = np.ascontiguousarray(
        np.concatenate([np.asarray(Wg, f32), np.ones((D, 1), f32)], axis=1))
    cst["w1"] = np.ascontiguousarray(np.asarray(W1, f32))
    cst["w2"] = np.ascontiguousarray(np.asarray(W2, f32))
    cst["bgb"] = np.ascontiguousarray(np.tile(np.asarray(bg, f32)[None, :], (128, 1)))
    cst["ident"] = np.eye(128, dtype=f32)
    cst["ltriT"] = np.triu(np.ones((128, 128), f32), 1)
    cst["iota8"] = np.ascontiguousarray(
        np.tile(np.arange(E, dtype=f32)[None, :], (128, 1)))
    cst["onesc"] = np.ones((128, 1), f32)
    cst["onesr"] = np.ones((1, 128), f32)
    cst["e512r"] = (np.arange(E, dtype=f32) * CAP).reshape(1, E)
    slopes32 = np.asarray([2.0 ** (-8.0 * (j + 1) / H) for j in range(H)], f32)
    pp = np.arange(128)[:, None]
    ccp = np.arange(2176)[None, :]
    dist = (pp + 2047 - ccp).astype(f32)
    biasm = np.zeros((H, 128, 2176), f32)
    for h in range(H):
        bm = (-slopes32[h]) * dist
        bm[dist < 0] = NEGBIG
        biasm[h] = bm
    cst["biasm"] = biasm

    in_maps = []
    for core in range(B):
        m = dict(cst)
        m["x_own"] = np.ascontiguousarray(x[core])
        order = np.roll(np.arange(B), -core)
        m["xTf"] = np.ascontiguousarray(x[order].reshape(B * T, D).T)
        in_maps.append(m)
    return in_maps


def kernel(L0_attn_x, L0_attn_Wq, L0_attn_Wk, L0_attn_Wv, L0_attn_Wo,
           L0_ffn_Wg, L0_ffn_bg, L0_ffn_W1, L0_ffn_W2):
    in_maps = _host_inputs(L0_attn_x, L0_attn_Wq, L0_attn_Wk, L0_attn_Wv,
                           L0_attn_Wo, L0_ffn_Wg, L0_ffn_bg, L0_ffn_W1, L0_ffn_W2)
    nc = build_nc()
    res = run_bass_kernel_spmd(nc, in_maps, core_ids=list(range(B)))
    out = np.stack([np.asarray(r["out"], dtype=np.float32) for r in res.results])
    return out


if __name__ == "__main__":
    import time
    t0 = time.time()
    build_nc()
    print(f"built in {time.time() - t0:.1f}s")
